# version 10
# baseline (speedup 1.0000x reference)
import os
import time
import numpy as np
import ml_dtypes
import concourse.bass as bass
import concourse.mybir as mybir
from concourse.bass_utils import run_bass_kernel_spmd
from contextlib import ExitStack

F32 = mybir.dt.float32
BF16 = mybir.dt.bfloat16
I32 = mybir.dt.int32
AF = mybir.ActivationFunctionType
ALU = mybir.AluOpType
AX = mybir.AxisListType

P = 128
D = 1024
NCH = 8
H = 16
DH = 64
NSUB_OWN = 17
NSUB_FOR = 16
QG = [(0, 512), (512, 512), (1024, 512), (1536, 512), (2048, 128)]
FG = [(128, 512), (640, 512), (1152, 512), (1664, 512)]
CHAIN = [("a", 0), ("a", 1), ("f", 0), ("a", 2), ("f", 1), ("a", 3), ("f", 2), ("a", 4), ("f", 3)]

ENGINES = ("gpsimd", "sync", "tensor", "vector", "scalar")
C_EPS, C_ZEPS, C_ONE, C_NEG1, C_NEGH, C_I1024, C_QTR, C_ZERO = range(8)


def build_program():
    nc = bass.Bass("TRN2", target_bir_lowering=False)

    xin = nc.dram_tensor("xin", [NSUB_OWN * P, D], F32, kind="ExternalInput")
    xfor = nc.dram_tensor("xfor", [NSUB_FOR * P, D], F32, kind="ExternalInput")
    w_dr = {}
    for w in ("wq", "wk", "wv", "wo", "wrt", "wkt", "wvt"):
        w_dr[w] = nc.dram_tensor(w, [D, D], F32, kind="ExternalInput")
    bqs_d = nc.dram_tensor("bqs", [P, NCH], F32, kind="ExternalInput")
    bkb_d = nc.dram_tensor("bkb", [P, D], BF16, kind="ExternalInput")
    bvb_d = nc.dram_tensor("bvb", [P, D], BF16, kind="ExternalInput")
    bob_d = nc.dram_tensor("bob", [P, D], BF16, kind="ExternalInput")
    murs_d = nc.dram_tensor("murs", [P, NCH], F32, kind="ExternalInput")
    muks_d = nc.dram_tensor("muks", [P, NCH], F32, kind="ExternalInput")
    sel_d = nc.dram_tensor("sel", [H, D], BF16, kind="ExternalInput")
    ones_d = nc.dram_tensor("ones1", [P, 1], BF16, kind="ExternalInput")
    cst_d = nc.dram_tensor("cstin", [P, 8], F32, kind="ExternalInput")
    hmask_d = nc.dram_tensor("hmask", [P, 1], F32, kind="ExternalInput")
    meta_d = nc.dram_tensor("meta", [1, 4], I32, kind="ExternalInput")

    out_d = nc.dram_tensor("out", [NSUB_FOR * P, D], F32, kind="ExternalOutput")

    xnew_d = nc.dram_tensor("xnew_d", [NSUB_OWN * P, D], F32)
    shr = nc.dram_tensor("shr", [2, P, 520], F32, addr_space="Shared")
    flg = nc.dram_tensor("flg", [1, 64], I32, addr_space="Shared")

    ctx = ExitStack()
    with ctx:
        sb = lambda name, shape, dt: ctx.enter_context(nc.sbuf_tensor(name, shape, dt))
        ps = lambda name, shape, dt: ctx.enter_context(nc.psum_tensor(name, shape, dt))
        mksem = lambda name: ctx.enter_context(nc.semaphore(name))

        cst = sb("cst", [P, 8], F32)
        bqs = sb("bqs_s", [P, NCH], F32)
        murs = sb("murs_s", [P, NCH], F32)
        muks = sb("muks_s", [P, NCH], F32)
        hmask = sb("hmask_s", [P, 1], F32)
        bkb = sb("bkb_s", [P, D], BF16)
        bvb = sb("bvb_s", [P, D], BF16)
        bob = sb("bob_s", [P, D], BF16)
        sel = sb("sel_s", [H, D], BF16)
        ones1 = sb("ones1_s", [P, 1], BF16)
        mv = sb("mv", [1, 4], I32)
        fv = sb("fv", [1, 64], I32)

        wsl = [sb(f"wsl{i}", [P, NCH, D], BF16) for i in range(4)]
        x1tq = sb("x1tq", [P, NCH, 2688], BF16)
        scores = sb("scores", [P, NCH, 512], BF16)
        xsp = [sb(f"xsp{i}", [P, D], F32) for i in range(2)]
        kbp = [sb(f"kbp{i}", [P, D], F32) for i in range(2)]
        x1p = [sb(f"x1p{i}", [P, D], BF16) for i in range(2)]
        ktok = [sb(f"ktok{i}", [P, D], BF16) for i in range(2)]
        vtok = [sb(f"vtok{i}", [P, D], BF16) for i in range(2)]
        ep = [sb(f"ep{i}", [P, D], BF16) for i in range(2)]
        rp = [sb(f"rp{i}", [P, D], BF16) for i in range(2)]
        xfT = [sb(f"xfT{i}", [P, NCH, P], BF16) for i in range(2)]
        xr = sb("xr", [P, NCH, 512], BF16)
        xk = sb("xk", [P, NCH, 512], BF16)
        xnw = sb("xnw", [P, D], F32)
        xnb = sb("xnb", [P, D], F32)
        asb = sb("asb", [P, D], F32)
        ofb = sb("ofb", [P, D], F32)
        outb = sb("outb", [P, D], F32)
        kvm = sb("kvm", [P, 520], F32)
        kvp = sb("kvp", [P, 520], F32)
        kvblk = sb("kvblk", [P, NCH, P], BF16)
        mks = sb("mks", [P, NCH, H], BF16)
        zrb = sb("zrb", [H, 512], BF16)
        st12 = [sb(f"st12{i}", [P, 12], F32) for i in range(2)]
        m4 = [sb(f"m4{i}", [P, 4], F32) for i in range(2)]
        ast = [sb(f"ast{i}", [P, 16], F32) for i in range(2)]

        pk = ps("pk", [P, D], F32)
        pv = ps("pv", [P, D], F32)
        pkv = ps("pkv", [P, 512], F32)
        pks = ps("pks", [P, 512], F32)
        pr = ps("pr", [P, D], F32)

        sem_names = ["sLD", "sTR", "sX1", "sKP", "sVP", "sKB", "sER", "sKT",
                     "sVT", "sKV", "sSC", "sEX", "sPL", "sQP", "sQE", "sQD",
                     "sDEN", "sZ", "sZB", "sQ2", "sSCO", "sSCE", "sOP", "sAE",
                     "sSQ", "sVS", "sSR", "sXN", "sX2", "sT2", "sMX", "sRP",
                     "sSG", "sKF", "sKR", "sKA", "sV2", "sOF", "sOU", "sST",
                     "sWL", "sXW"]
        SEM = {n: mksem(n) for n in sem_names}

        ops = {e: [] for e in ENGINES}
        counts = {n: 0 for n in sem_names}
        waited = {}

        def emit(eng, fn):
            ops[eng].append(fn)

        def wait(eng, semn, tgt):
            if tgt <= 0:
                return
            key = (eng, semn)
            if waited.get(key, 0) >= tgt:
                return
            waited[key] = tgt
            emit(eng, lambda e, s=semn, t=tgt: e.wait_ge(SEM[s], t))

        def inc(eng, semn, n, builder, dma=False):
            counts[semn] += n
            if eng in ("vector", "scalar") and not dma:
                emit(eng, builder)
                emit(eng, lambda e: e.drain())
                emit(eng, lambda e, s=semn, m=n: e.nop().then_inc(SEM[s], m))
            else:
                emit(eng, lambda e, b=builder, s=semn, m=n: b(e).then_inc(SEM[s], m))
            return counts[semn]

        raw = emit

        def vdrain():
            emit("vector", lambda e: e.drain())

        def sdrain():
            emit("scalar", lambda e: e.drain())

        def x1T(c, c0, w):
            return x1tq[:, c, 512 + c0:512 + c0 + w]

        def qv(c, c0, w):
            return x1tq[:, c, c0:c0 + w]

        for sbuf_t, dr in [(cst, cst_d), (bqs, bqs_d), (murs, murs_d),
                           (muks, muks_d), (hmask, hmask_d), (bkb, bkb_d),
                           (bvb, bvb_d), (bob, bob_d), (sel, sel_d),
                           (ones1, ones_d), (mv, meta_d)]:
            inc("gpsimd", "sLD", 16, lambda e, s=sbuf_t, d=dr: e.dma_start(s[:, :], d[:, :]))
        ld_const = counts["sLD"]

        wl = {}
        for slot, wname in ((0, "wk"), (1, "wv"), (2, "wq"), (3, "wkt")):
            wl[wname] = inc("gpsimd", "sWL", 16,
                            lambda e, s=slot, w=wname: e.dma_start(
                                wsl[s][:, :, :],
                                w_dr[w].rearrange("(c p) d -> p c d", p=P)))

        units = [("own", sx) for sx in range(NSUB_OWN)] + [("for", t) for t in range(NSUB_FOR)]
        NU = len(units)
        has_kv = lambda u: not (units[u][0] == "own" and units[u][1] == 0)
        kv_units = [u for u in range(NU) if has_kv(u)]

        ld_tok, x1_tok, tr_tok = {}, {}, {}
        kp_tok, vp_tok, kb_tok, er_tok, kt_tok, vt_tok, kv_tok = {}, {}, {}, {}, {}, {}, {}

        def frontA(u):
            kind, sx = units[u]
            b = u % 2
            src = xin if kind == "own" else xfor
            row0 = sx * P
            if u >= 2:
                wait("gpsimd", "sX1", x1_tok[u - 2])
            ld_tok[u] = inc("gpsimd", "sLD", 16,
                            lambda e, sr=src, r=row0, b=b: e.dma_start(xsp[b][:, :], sr[r:r + P, :]))
            wait("vector", "sLD", ld_tok[u])
            wait("vector", "sLD", ld_const)
            raw("vector", lambda e, b=b: e.bn_stats(st12[b][:, 0:6], xsp[b][:, 0:512]))
            raw("vector", lambda e, b=b: e.bn_stats(st12[b][:, 6:12], xsp[b][:, 512:1024]))
            vdrain()
            raw("vector", lambda e, b=b: e.tensor_mul(m4[b][:, :], st12[b][:, 1:12:3], st12[b][:, 1:12:3]))
            raw("vector", lambda e, b=b: e.reduce_sum(ast[b][:, 0:1], st12[b][:, 1:12:3], axis=AX.X))
            raw("vector", lambda e, b=b: e.reduce_sum(ast[b][:, 1:2], m4[b][:, :], axis=AX.X))
            raw("vector", lambda e, b=b: e.reduce_sum(ast[b][:, 2:3], st12[b][:, 2:12:3], axis=AX.X))
            raw("vector", lambda e, b=b: e.tensor_scalar(ast[b][:, 3:4], ast[b][:, 0:1], cst[:, C_QTR:C_QTR + 1], None, ALU.mult))
            raw("vector", lambda e, b=b: e.tensor_scalar(ast[b][:, 4:5], ast[b][:, 1:2], cst[:, C_QTR:C_QTR + 1], None, ALU.mult))
            raw("vector", lambda e, b=b: e.tensor_scalar(ast[b][:, 5:6], ast[b][:, 2:3], cst[:, C_I1024:C_I1024 + 1], None, ALU.mult))
            vdrain()
            raw("vector", lambda e, b=b: e.tensor_add(ast[b][:, 6:7], ast[b][:, 4:5], ast[b][:, 5:6]))
            raw("vector", lambda e, b=b: e.tensor_mul(ast[b][:, 7:8], ast[b][:, 3:4], ast[b][:, 3:4]))
            vdrain()
            vs_t = inc("vector", "sVS", 1, lambda e, b=b: e.tensor_sub(ast[b][:, 8:9], ast[b][:, 6:7], ast[b][:, 7:8]))
            wait("scalar", "sVS", vs_t)
            raw("scalar", lambda e, b=b: e.activation(ast[b][:, 9:10], ast[b][:, 8:9], AF.Ln, bias=cst[:, C_EPS:C_EPS + 1]))
            sdrain()
            sr_t = inc("scalar", "sSR", 1, lambda e, b=b: e.activation(ast[b][:, 10:11], ast[b][:, 9:10], AF.Exp, scale=cst[:, C_NEGH:C_NEGH + 1]))
            wait("vector", "sSR", sr_t)
            if u >= 2:
                wait("vector", "sTR", tr_tok[u - 2])
            x1_tok[u] = inc("vector", "sX1", 1,
                            lambda e, b=b: e.tensor_scalar(x1p[b][:, :], xsp[b][:, :],
                                                           ast[b][:, 3:4], ast[b][:, 10:11],
                                                           ALU.subtract, ALU.mult))
            wait("sync", "sX1", x1_tok[u])
            if kind == "own":
                dst = x1tq[:, :, 512 + sx * P:512 + sx * P + P]
            else:
                if u >= 2 and units[u - 2][0] == "for":
                    wait("sync", "sVP", vp_tok[u - 2])
                dst = xfT[b][:, :, :]
            wait("sync", "sTR", counts["sTR"])
            tr_tok[u] = inc("sync", "sTR", 16,
                            lambda e, d=dst, b=b: e.dma_start(d, x1p[b][:, :], transpose=True))
            if not has_kv(u):
                return
            if kind == "own":
                lhs = (lambda c, sx=sx: x1T(c, sx * P, P))
            else:
                lhs = (lambda c, b=b: xfT[b][:, c, :])
            prev_kv = [x for x in kv_units if x < u]
            wait("tensor", "sTR", tr_tok[u])
            wait("tensor", "sWL", max(wl["wk"], wl["wv"]))
            if prev_kv:
                wait("tensor", "sKB", kb_tok[prev_kv[-1]])
            for n in range(2):
                for c in range(NCH):
                    bld = lambda e, c=c, n=n, lh=lhs: e.matmul(
                        pk[:, n * 512:(n + 1) * 512], lh(c),
                        wsl[0][:, c, n * 512:(n + 1) * 512],
                        start=(c == 0), stop=(c == NCH - 1))
                    if n == 1 and c == NCH - 1:
                        kp_tok[u] = inc("tensor", "sKP", 1, bld)
                    else:
                        raw("tensor", bld)
            if prev_kv:
                wait("tensor", "sVT", vt_tok[prev_kv[-1]])
            for n in range(2):
                for c in range(NCH):
                    bld = lambda e, c=c, n=n, lh=lhs: e.matmul(
                        pv[:, n * 512:(n + 1) * 512], lh(c),
                        wsl[1][:, c, n * 512:(n + 1) * 512],
                        start=(c == 0), stop=(c == NCH - 1))
                    if n == 1 and c == NCH - 1:
                        vp_tok[u] = inc("tensor", "sVP", 1, bld)
                    else:
                        raw("tensor", bld)

        def back1A(u):
            if not has_kv(u):
                return
            b = u % 2
            prev_kv = [x for x in kv_units if x < u]
            prev_par = [x for x in prev_kv if x % 2 == b]
            if prev_par:
                wait("vector", "sER", er_tok[prev_par[-1]])
            wait("vector", "sKP", kp_tok[u])
            kb_tok[u] = inc("vector", "sKB", 1,
                            lambda e, b=b: e.tensor_add(kbp[b][:, :], pk[:, :], bkb[:, :]))
            wait("vector", "sVP", vp_tok[u])
            if prev_par:
                wait("vector", "sKV", kv_tok[prev_par[-1]])
            vt_tok[u] = inc("vector", "sVT", 1,
                            lambda e, b=b: e.tensor_add(vtok[b][:, :], pv[:, :], bvb[:, :]))
            if prev_par:
                wait("scalar", "sKT", kt_tok[prev_par[-1]])
            wait("scalar", "sKB", kb_tok[u])
            raw("scalar", lambda e, b=b: e.activation(ep[b][:, :], kbp[b][:, :], AF.Exp))
            er_tok[u] = inc("scalar", "sER", 1,
                            lambda e, b=b: e.activation(rp[b][:, :], kbp[b][:, :], AF.Relu))

        def back2A(u):
            if not has_kv(u):
                return
            b = u % 2
            wait("vector", "sER", er_tok[u])
            kt_tok[u] = inc("vector", "sKT", 1,
                            lambda e, b=b: e.scalar_tensor_tensor(
                                ktok[b][:, :], ep[b][:, :], cst[:, C_ONE:C_ONE + 1],
                                rp[b][:, :], ALU.min, ALU.add))
            wait("tensor", "sKT", kt_tok[u])
            wait("tensor", "sVT", vt_tok[u])
            first = (u == kv_units[0])
            last_u = (u == kv_units[-1])
            for h in range(H):
                po, co = 64 * (h % 2), 64 * (h // 2)
                raw("tensor", lambda e, b=b, h=h, po=po, co=co,
                    f=(first and h == 0), l=(last_u and h == H - 1): e.matmul(
                    pkv[po:po + 64, co:co + 64],
                    ktok[b][:, 64 * h:64 * h + 64], vtok[b][:, 64 * h:64 * h + 64],
                    start=f, stop=l, skip_group_check=True))
            for c in range(NCH):
                bld = lambda e, b=b, c=c, f=(first and c == 0), l=(last_u and c == NCH - 1): e.matmul(
                    pks[:, c:c + 1], ktok[b][:, c * P:(c + 1) * P], ones1[:, :],
                    start=f, stop=l, skip_group_check=True)
                if c == NCH - 1:
                    kv_tok[u] = inc("tensor", "sKV", 1, bld)
                else:
                    raw("tensor", bld)

        for step in range(NU + 2):
            if 2 <= step and step - 2 < NU:
                back2A(step - 2)
            if 1 <= step and step - 1 < NU:
                back1A(step - 1)
            if step < NU:
                frontA(step)

        wait("scalar", "sKV", counts["sKV"])
        raw("scalar", lambda e: e.copy(kvm[:, 0:512], pkv[:, :]))
        sc_t = inc("scalar", "sSC", 1, lambda e: e.copy(kvm[:, 512:520], pks[:, 0:8]))

        wait("gpsimd", "sSC", sc_t)
        wait("gpsimd", "sLD", ld_const)

        def exchange(g):
            with (g.register("sd") as sd, g.register("n0") as n0,
                  g.register("n1") as n1, g.register("off") as off,
                  g.register("acc") as acc, g.register("t") as t,
                  g.register("pc") as pc):
                g.reg_load(sd, mv[:1, 0:1])
                g.reg_load(n0, mv[:1, 1:2])
                g.reg_load(n1, mv[:1, 2:3])
                g.reg_mul(off, sd, P * 520)
                g.dma_start(bass.AP(shr, off, [[520, P], [1, 520]]), kvm[:, :]).then_inc(SEM["sPL"], 16)
                g.wait_ge(SEM["sPL"], 16)
                g.reg_mul(off, sd, 2)
                g.dma_start(bass.AP(flg, off, [[2, 1], [1, 2]]), mv[:1, 1:3]).then_inc(SEM["sPL"], 16)
                g.wait_ge(SEM["sPL"], 32)
                g.reg_mov(acc, 1)
                g.reg_mov(pc, 32)
                with g.While(acc):
                    g.reg_add(pc, pc, 16)
                    g.dma_start(fv[:, :], flg[:, :]).then_inc(SEM["sPL"], 16)
                    g.wait_ge(SEM["sPL"], pc)
                    with g.register("m0") as m0, g.register("m1") as m1:
                        g.reg_mov(m0, 0)
                        g.reg_load(t, fv[0:1, 0:1])
                        g.reg_alu(t, t, n0, ALU.bitwise_xor)
                        g.reg_alu(m0, m0, t, ALU.bitwise_or)
                        g.reg_load(t, fv[0:1, 1:2])
                        g.reg_alu(t, t, n1, ALU.bitwise_xor)
                        g.reg_alu(m0, m0, t, ALU.bitwise_or)
                        g.reg_mov(m1, 0)
                        g.reg_load(t, fv[0:1, 2:3])
                        g.reg_alu(t, t, n0, ALU.bitwise_xor)
                        g.reg_alu(m1, m1, t, ALU.bitwise_or)
                        g.reg_load(t, fv[0:1, 3:4])
                        g.reg_alu(t, t, n1, ALU.bitwise_xor)
                        g.reg_alu(m1, m1, t, ALU.bitwise_or)
                        g.reg_mul(m0, m0, sd)
                        g.reg_mov(t, 1)
                        g.reg_sub(t, t, sd)
                        g.reg_mul(m1, m1, t)
                        g.reg_mov(acc, 0)
                        g.reg_alu(acc, acc, m0, ALU.bitwise_or)
                        g.reg_alu(acc, acc, m1, ALU.bitwise_or)
                g.reg_mov(t, 1)
                g.reg_sub(t, t, sd)
                g.reg_mul(off, t, P * 520)
                g.reg_add(pc, pc, 16)
                g.dma_start(kvp[:, :], bass.AP(shr, off, [[520, P], [1, 520]])).then_inc(SEM["sPL"], 16)
                g.wait_ge(SEM["sPL"], pc)

        raw("gpsimd", exchange)
        ex_t = inc("gpsimd", "sEX", 1, lambda e: e.nop())

        wait("vector", "sEX", ex_t)
        raw("vector", lambda e: e.tensor_add(kvm[:, :], kvm[:, :], kvp[:, :]))
        raw("vector", lambda e: e.memset(kvblk[:, :, :], 0))
        raw("vector", lambda e: e.memset(mks[:, :, :], 0))
        for c in range(NCH):
            raw("vector", lambda e, c=c: e.tensor_copy(kvblk[0:64, c, 0:64], kvm[0:64, 64 * c:64 * c + 64]))
            raw("vector", lambda e, c=c: e.tensor_copy(kvblk[64:128, c, 64:128], kvm[64:128, 64 * c:64 * c + 64]))
            raw("vector", lambda e, c=c: e.tensor_copy(mks[0:64, c, 2 * c:2 * c + 1], kvm[0:64, 512 + c:513 + c]))
        kva_t = None
        for c in range(NCH):
            bld = lambda e, c=c: e.tensor_copy(mks[64:128, c, 2 * c + 1:2 * c + 2], kvm[64:128, 512 + c:513 + c])
            if c == NCH - 1:
                kva_t = inc("vector", "sEX", 1, bld)
            else:
                raw("vector", bld)

        wait("gpsimd", "sKV", counts["sKV"])
        wl["wo"] = inc("gpsimd", "sWL", 16,
                       lambda e: e.dma_start(wsl[0][:, :, :],
                                             w_dr["wo"].rearrange("(c p) d -> p c d", p=P)))
        wl["wrt"] = inc("gpsimd", "sWL", 16,
                        lambda e: e.dma_start(wsl[1][:, :, :],
                                              w_dr["wrt"].rearrange("(c p) d -> p c d", p=P)))

        qd_tok, q2_tok = {}, {}
        first_q = True
        for g, (c0, W) in enumerate(QG):
            for dq in range(NCH):
                half = dq % 2
                wait("tensor", "sQE", counts["sQE"] - 1)
                if first_q:
                    wait("tensor", "sTR", counts["sTR"])
                    wait("tensor", "sWL", wl["wq"])
                    wait("tensor", "sKB", counts["sKB"])
                for c in range(NCH):
                    bld = lambda e, c=c, dq=dq, half=half, c0=c0, W=W: e.matmul(
                        pk[:, half * 512:half * 512 + W],
                        wsl[2][:, c, dq * P:(dq + 1) * P],
                        x1T(c, c0, W),
                        start=(c == 0), stop=(c == NCH - 1))
                    if c == NCH - 1:
                        qp_t = inc("tensor", "sQP", 1, bld)
                    else:
                        raw("tensor", bld)
                if first_q:
                    wait("scalar", "sKT", counts["sKT"])
                    first_q = False
                else:
                    wait("scalar", "sQD", counts["sQD"] - 1)
                wait("scalar", "sQP", qp_t)
                raw("scalar", lambda e, half=half, W=W, dq=dq: e.activation(
                    ep[half][:, 0:W], pk[:, half * 512:half * 512 + W], AF.Exp,
                    bias=bqs[:, dq:dq + 1]))
                qe_t = inc("scalar", "sQE", 1, lambda e, half=half, W=W, dq=dq: e.activation(
                    rp[half][:, 0:W], pk[:, half * 512:half * 512 + W], AF.Relu,
                    bias=bqs[:, dq:dq + 1]))
                if g == 0 and dq == 0:
                    wait("vector", "sKT", counts["sKT"])
                wait("vector", "sQE", qe_t)
                qd_tok[(g, dq)] = inc("vector", "sQD", 1,
                                      lambda e, dq=dq, half=half, c0=c0, W=W: e.scalar_tensor_tensor(
                                          qv(dq, c0, W), ep[half][:, 0:W], cst[:, C_ONE:C_ONE + 1],
                                          rp[half][:, 0:W], ALU.min, ALU.add))
            for dq in range(NCH):
                wait("tensor", "sQD", qd_tok[(g, dq)])
            if g == 0:
                wait("tensor", "sSC", sc_t)
                wait("tensor", "sEX", kva_t)
            else:
                wait("tensor", "sZ", counts["sZ"])
            for c in range(NCH):
                bld = lambda e, c=c, c0=c0, W=W: e.matmul(
                    pks[0:16, 0:W], mks[:, c, :], qv(c, c0, W),
                    start=(c == 0), stop=(c == NCH - 1))
                if c == NCH - 1:
                    den_t = inc("tensor", "sDEN", 1, bld)
                else:
                    raw("tensor", bld)
            wait("scalar", "sDEN", den_t)
            raw("scalar", lambda e, W=W: e.activation(kvp[0:16, 0:W], pks[0:16, 0:W], AF.Ln,
                                                      bias=cst[0:16, C_ZEPS:C_ZEPS + 1]))
            sdrain()
            z_t = inc("scalar", "sZ", 1, lambda e, W=W: e.activation(
                kvp[32:48, 0:W], kvp[0:16, 0:W], AF.Exp, scale=cst[0:16, C_NEG1:C_NEG1 + 1]))
            wait("vector", "sZ", z_t)
            wait("vector", "sZB", counts["sZB"])
            zc_t = inc("vector", "sZ", 1, lambda e, W=W: e.tensor_copy(zrb[:, 0:W], kvp[32:48, 0:W]))
            for c in range(NCH):
                half = c % 2
                wait("tensor", "sQ2", counts["sQ2"] - 1)
                wait("tensor", "sZ", zc_t)
                zb_t = inc("tensor", "sZB", 1,
                           lambda e, c=c, half=half, W=W: e.matmul(
                               pr[:, half * 512:half * 512 + W], sel[:, c * P:(c + 1) * P],
                               zrb[:, 0:W], start=True, stop=True))
                wait("vector", "sZB", zb_t)
                q2_tok[(g, c)] = inc("vector", "sQ2", 1,
                                     lambda e, c=c, half=half, c0=c0, W=W: e.tensor_mul(
                                         qv(c, c0, W), qv(c, c0, W),
                                         pr[:, half * 512:half * 512 + W]))

        wait("gpsimd", "sQP", counts["sQP"])
        wl["wvt"] = inc("gpsimd", "sWL", 16,
                        lambda e: e.dma_start(wsl[2][:, :, :],
                                              w_dr["wvt"].rearrange("(c p) d -> p c d", p=P)))

        x2_ready = {}
        xn_stored = {}
        xn_tok = {}
        st_tok = {}
        x2halo = None
        f_done = {}
        first_a = True
        first_x2 = True
        ast_i = [0]

        for phase, gi in CHAIN:
            if phase == "a":
                c0, W = QG[gi]
                subs = [c0 // P + j for j in range(W // P)]
                for c in range(NCH):
                    half = c % 2
                    wait("tensor", "sSCE", counts["sSCE"] - 1)
                    if first_a and c == 0:
                        wait("tensor", "sQE", counts["sQE"])
                    wait("tensor", "sQ2", q2_tok[(gi, c)])
                    sco_t = inc("tensor", "sSCO", 1,
                                lambda e, c=c, half=half, c0=c0, W=W: e.matmul(
                                    pk[:, half * 512:half * 512 + W], kvblk[:, c, :],
                                    qv(c, c0, W), start=True, stop=True))
                    if not first_a and c == 0:
                        wait("scalar", "sOP", counts["sOP"])
                    wait("scalar", "sSCO", sco_t)
                    sce_t = inc("scalar", "sSCE", 1,
                                lambda e, c=c, half=half, W=W: e.copy(
                                    scores[:, c, 0:W], pk[:, half * 512:half * 512 + W]))
                for j, su in enumerate(subs):
                    ai = ast_i[0] % 2
                    ast_i[0] += 1
                    if first_a and j == 0:
                        wait("scalar", "sX1", counts["sX1"])
                    prev_su = su - 2
                    if prev_su in xn_tok:
                        wait("scalar", "sXN", xn_tok[prev_su])
                    ld_b = inc("scalar", "sLD", 16,
                               lambda e, su=su, bb=su % 2: e.dma_start(
                                   xsp[bb][:, :], xin[su * P:su * P + P, :]),
                               dma=True)
                    wait("tensor", "sSCE", sce_t)
                    if first_a and j == 0:
                        wait("tensor", "sWL", wl["wo"])
                        wait("tensor", "sVT", counts["sVT"])
                        first_a = False
                    else:
                        wait("tensor", "sAE", counts["sAE"])
                    for n in range(2):
                        for c in range(NCH):
                            bld = lambda e, c=c, n=n, j=j: e.matmul(
                                pv[:, n * 512:(n + 1) * 512], scores[:, c, j * P:(j + 1) * P],
                                wsl[0][:, c, n * 512:(n + 1) * 512],
                                start=(c == 0), stop=(c == NCH - 1))
                            if n == 1 and c == NCH - 1:
                                op_t = inc("tensor", "sOP", 1, bld)
                            else:
                                raw("tensor", bld)
                    wait("vector", "sSR", counts["sSR"])
                    wait("vector", "sSQ", counts["sSQ"])
                    wait("vector", "sOP", op_t)
                    ae_t = inc("vector", "sAE", 1,
                               lambda e, ai=ai: e.scalar_tensor_tensor(
                                   asb[:, :], pv[:, :], cst[:, C_ONE:C_ONE + 1],
                                   bob[:, :], ALU.mult, ALU.add,
                                   accum_out=ast[ai][:, 0:1]))
                    wait("scalar", "sAE", ae_t)
                    sq_t = inc("scalar", "sSQ", 1,
                               lambda e, ai=ai: e.activation(ktok[0][:, :], asb[:, :], AF.Square,
                                                             accum_out=ast[ai][:, 1:2]))
                    wait("vector", "sSQ", sq_t)
                    vdrain()
                    raw("vector", lambda e, ai=ai: e.tensor_scalar(ast[ai][:, 2:3], ast[ai][:, 0:1], cst[:, C_I1024:C_I1024 + 1], None, ALU.mult))
                    raw("vector", lambda e, ai=ai: e.tensor_scalar(ast[ai][:, 3:4], ast[ai][:, 1:2], cst[:, C_I1024:C_I1024 + 1], None, ALU.mult))
                    raw("vector", lambda e, ai=ai: e.tensor_mul(ast[ai][:, 4:5], ast[ai][:, 2:3], ast[ai][:, 2:3]))
                    vdrain()
                    vs_t = inc("vector", "sVS", 1, lambda e, ai=ai: e.tensor_sub(ast[ai][:, 5:6], ast[ai][:, 3:4], ast[ai][:, 4:5]))
                    wait("scalar", "sVS", vs_t)
                    raw("scalar", lambda e, ai=ai: e.activation(ast[ai][:, 6:7], ast[ai][:, 5:6], AF.Ln, bias=cst[:, C_EPS:C_EPS + 1]))
                    sdrain()
                    sr_t = inc("scalar", "sSR", 1, lambda e, ai=ai: e.activation(ast[ai][:, 7:8], ast[ai][:, 6:7], AF.Exp, scale=cst[:, C_NEGH:C_NEGH + 1]))
                    wait("vector", "sSR", sr_t)
                    if su - 1 in xn_stored:
                        wait("vector", "sXW", xn_stored[su - 1])
                    raw("vector", lambda e, ai=ai: e.tensor_scalar(
                        xnw[:, :], asb[:, :], ast[ai][:, 2:3], ast[ai][:, 7:8],
                        ALU.subtract, ALU.mult))
                    vdrain()
                    wait("vector", "sLD", ld_b)
                    xn_tok[su] = inc("vector", "sXN", 1,
                                     lambda e, ai=ai, bb=su % 2: e.scalar_tensor_tensor(
                                         xnw[:, :], xnw[:, :], cst[:, C_ONE:C_ONE + 1],
                                         xsp[bb][:, :], ALU.mult, ALU.add,
                                         accum_out=ast[ai][:, 8:9]))
                    wait("scalar", "sXN", xn_tok[su])
                    xn_stored[su] = inc("scalar", "sXW", 16,
                                        lambda e, su=su: e.dma_start(
                                            xnew_d[su * P:su * P + P, :], xnw[:, :]),
                                        dma=True)
                    wait("scalar", "sXN", xn_tok[su])
                    sq2_t = inc("scalar", "sSQ", 1,
                                lambda e, ai=ai: e.activation(ktok[0][:, :], xnw[:, :], AF.Square,
                                                              accum_out=ast[ai][:, 9:10]))
                    wait("vector", "sSQ", sq2_t)
                    vdrain()
                    raw("vector", lambda e, ai=ai: e.tensor_scalar(ast[ai][:, 10:11], ast[ai][:, 8:9], cst[:, C_I1024:C_I1024 + 1], None, ALU.mult))
                    raw("vector", lambda e, ai=ai: e.tensor_scalar(ast[ai][:, 11:12], ast[ai][:, 9:10], cst[:, C_I1024:C_I1024 + 1], None, ALU.mult))
                    raw("vector", lambda e, ai=ai: e.tensor_mul(ast[ai][:, 12:13], ast[ai][:, 10:11], ast[ai][:, 10:11]))
                    vdrain()
                    vs2_t = inc("vector", "sVS", 1, lambda e, ai=ai: e.tensor_sub(ast[ai][:, 13:14], ast[ai][:, 11:12], ast[ai][:, 12:13]))
                    wait("scalar", "sVS", vs2_t)
                    raw("scalar", lambda e, ai=ai: e.activation(ast[ai][:, 14:15], ast[ai][:, 13:14], AF.Ln, bias=cst[:, C_EPS:C_EPS + 1]))
                    sdrain()
                    sr2_t = inc("scalar", "sSR", 1, lambda e, ai=ai: e.activation(ast[ai][:, 15:16], ast[ai][:, 14:15], AF.Exp, scale=cst[:, C_NEGH:C_NEGH + 1]))
                    wait("vector", "sSR", sr2_t)
                    if first_x2:
                        wait("vector", "sTR", counts["sTR"])
                        first_x2 = False
                    if su - 2 in x2_ready:
                        wait("vector", "sT2", x2_ready[su - 2])
                    x2_tk = inc("vector", "sX2", 1,
                                lambda e, ai=ai, bb=su % 2: e.tensor_scalar(
                                    x1p[bb][:, :], xnw[:, :], ast[ai][:, 10:11], ast[ai][:, 15:16],
                                    ALU.subtract, ALU.mult))
                    wait("sync", "sX2", x2_tk)
                    wait("sync", "sSCO", counts["sSCO"])
                    wait("sync", "sT2", counts["sT2"])
                    x2_ready[su] = inc("sync", "sT2", 16,
                                       lambda e, su=su, bb=su % 2: e.dma_start(
                                           x1tq[:, :, su * P:su * P + P],
                                           x1p[bb][:, :], transpose=True))
                    if su == 0:
                        wait("vector", "sT2", x2_ready[0])
                        for c in range(NCH):
                            bld = lambda e, c=c: e.tensor_mul(x1tq[:, c, 127:128], x1tq[:, c, 127:128], hmask[:, :])
                            if c == NCH - 1:
                                x2halo = inc("vector", "sX2", 1, bld)
                            else:
                                raw("vector", bld)
            else:
                c0, W = FG[gi]
                subs = [c0 // P + j for j in range(W // P)]
                need = sorted(set(subs) | {(c0 - 1) // P})
                for s2 in need:
                    if s2 == 0:
                        wait("vector", "sX2", x2halo)
                    else:
                        wait("vector", "sT2", x2_ready[s2])
                if gi > 0:
                    wait("vector", "sV2", f_done[gi - 1])
                    wait("vector", "sKF", counts["sKF"])
                mx_t = None
                for c in range(NCH):
                    raw("vector", lambda e, c=c, c0=c0, W=W: e.tensor_sub(
                        xk[:, c, 0:W], x1tq[:, c, c0:c0 + W], x1tq[:, c, c0 - 1:c0 - 1 + W]))
                for c in range(NCH):
                    raw("vector", lambda e, c=c, c0=c0, W=W: e.scalar_tensor_tensor(
                        xr[:, c, 0:W], xk[:, c, 0:W], murs[:, c:c + 1],
                        x1tq[:, c, c0 - 1:c0 - 1 + W], ALU.mult, ALU.add))
                for c in range(NCH):
                    bld = lambda e, c=c, c0=c0, W=W: e.scalar_tensor_tensor(
                        xk[:, c, 0:W], xk[:, c, 0:W], muks[:, c:c + 1],
                        x1tq[:, c, c0 - 1:c0 - 1 + W], ALU.mult, ALU.add)
                    if c == NCH - 1:
                        mx_t = inc("vector", "sMX", 1, bld)
                    else:
                        raw("vector", bld)
                rp_tok_f = {}
                sg_tok = {}
                for j, su in enumerate(subs):
                    wait("tensor", "sMX", mx_t)
                    if gi == 0 and j == 0:
                        wait("tensor", "sWL", wl["wrt"])
                        wait("tensor", "sQ2", counts["sQ2"])
                    else:
                        wait("tensor", "sSG", counts["sSG"])
                        wait("tensor", "sOF", counts["sOF"])
                    for n in range(2):
                        for c in range(NCH):
                            bld = lambda e, c=c, n=n, j=j: e.matmul(
                                pr[:, n * 512:(n + 1) * 512], xr[:, c, j * P:(j + 1) * P],
                                wsl[1][:, c, n * 512:(n + 1) * 512],
                                start=(c == 0), stop=(c == NCH - 1))
                            if n == 1 and c == NCH - 1:
                                rp_tok_f[su] = inc("tensor", "sRP", 1, bld)
                            else:
                                raw("tensor", bld)
                    if gi == 0 and j == 0:
                        wait("scalar", "sQD", counts["sQD"])
                    wait("scalar", "sOF", counts["sOF"])
                    wait("scalar", "sRP", rp_tok_f[su])
                    rslot = (ep[0], ep[1], rp[0], rp[1])[j]
                    raw("scalar", lambda e: e.activation(kbp[0][:, :], pr[:, :], AF.Exp,
                                                         scale=cst[:, C_NEG1:C_NEG1 + 1]))
                    sdrain()
                    raw("scalar", lambda e: e.activation(kbp[1][:, :], kbp[0][:, :], AF.Ln,
                                                         bias=cst[:, C_ONE:C_ONE + 1]))
                    sdrain()
                    sg_tok[su] = inc("scalar", "sSG", 1,
                                     lambda e, r=rslot: e.activation(r[:, :], kbp[1][:, :], AF.Exp,
                                                                     scale=cst[:, C_NEG1:C_NEG1 + 1]))
                for dq in range(NCH):
                    pbank = pkv if dq % 2 == 0 else pks
                    wait("tensor", "sKA", counts["sKA"] - (1 if dq >= 2 else 0))
                    if gi == 0 and dq < 2:
                        wait("tensor", "sDEN", counts["sDEN"])
                        wait("tensor", "sZ", counts["sZ"])
                        wait("tensor", "sWL", wl["wkt"])
                    for c in range(NCH):
                        bld = lambda e, c=c, dq=dq, pb=pbank, W=W: e.matmul(
                            pb[:, 0:W], wsl[3][:, c, dq * P:(dq + 1) * P], xk[:, c, 0:W],
                            start=(c == 0), stop=(c == NCH - 1))
                        if c == NCH - 1:
                            kf_t = inc("tensor", "sKF", 1, bld)
                        else:
                            raw("tensor", bld)
                    wait("scalar", "sKA", counts["sKA"] - 1)
                    wait("scalar", "sKF", kf_t)
                    if gi == 0 and dq < 2:
                        wait("scalar", "sVT", counts["sVT"])
                    kr_t = inc("scalar", "sKR", 1,
                               lambda e, dq=dq, pb=pbank, W=W: e.activation(
                                   vtok[dq % 2][:, 0:W], pb[:, 0:W], AF.Relu))
                    wait("vector", "sKR", kr_t)
                    if dq == 0:
                        wait("vector", "sRP", rp_tok_f[subs[-1]])
                    bld = lambda e, dq=dq, W=W: e.tensor_mul(
                        xr[:, dq, 0:W], vtok[dq % 2][:, 0:W], vtok[dq % 2][:, 0:W])
                    ka_t = inc("vector", "sKA", 1, bld)
                for j, su in enumerate(subs):
                    ai = ast_i[0] % 2
                    ast_i[0] += 1
                    wait("tensor", "sKA", counts["sKA"])
                    wait("tensor", "sSG", sg_tok[su])
                    if gi == 0 and j == 0:
                        wait("tensor", "sWL", wl["wvt"])
                    else:
                        wait("tensor", "sOF", counts["sOF"])
                    for n in range(2):
                        for c in range(NCH):
                            bld = lambda e, c=c, n=n, j=j: e.matmul(
                                pr[:, n * 512:(n + 1) * 512], xr[:, c, j * P:(j + 1) * P],
                                wsl[2][:, c, n * 512:(n + 1) * 512],
                                start=(c == 0), stop=(c == NCH - 1))
                            if n == 1 and c == NCH - 1:
                                v2_t = inc("tensor", "sV2", 1, bld)
                            else:
                                raw("tensor", bld)
                    if j == len(subs) - 1:
                        f_done[gi] = v2_t
                    rslot = (ep[0], ep[1], rp[0], rp[1])[j]
                    wait("vector", "sSR", counts["sSR"])
                    wait("vector", "sSQ", counts["sSQ"])
                    wait("vector", "sV2", v2_t)
                    wait("vector", "sSG", sg_tok[su])
                    of_t = inc("vector", "sOF", 1,
                               lambda e, ai=ai, r=rslot: e.scalar_tensor_tensor(
                                   ofb[:, :], pr[:, :], cst[:, C_ONE:C_ONE + 1], r[:, :],
                                   ALU.mult, ALU.mult, accum_out=ast[ai][:, 0:1]))
                    wait("scalar", "sOF", of_t)
                    sq_t = inc("scalar", "sSQ", 1,
                               lambda e, ai=ai: e.activation(ktok[0][:, :], ofb[:, :], AF.Square,
                                                             accum_out=ast[ai][:, 1:2]))
                    wait("vector", "sSQ", sq_t)
                    vdrain()
                    raw("vector", lambda e, ai=ai: e.tensor_scalar(ast[ai][:, 2:3], ast[ai][:, 0:1], cst[:, C_I1024:C_I1024 + 1], None, ALU.mult))
                    raw("vector", lambda e, ai=ai: e.tensor_scalar(ast[ai][:, 3:4], ast[ai][:, 1:2], cst[:, C_I1024:C_I1024 + 1], None, ALU.mult))
                    raw("vector", lambda e, ai=ai: e.tensor_mul(ast[ai][:, 4:5], ast[ai][:, 2:3], ast[ai][:, 2:3]))
                    vdrain()
                    vs_t = inc("vector", "sVS", 1, lambda e, ai=ai: e.tensor_sub(ast[ai][:, 5:6], ast[ai][:, 3:4], ast[ai][:, 4:5]))
                    wait("scalar", "sVS", vs_t)
                    raw("scalar", lambda e, ai=ai: e.activation(ast[ai][:, 6:7], ast[ai][:, 5:6], AF.Ln, bias=cst[:, C_EPS:C_EPS + 1]))
                    sdrain()
                    sr_t = inc("scalar", "sSR", 1, lambda e, ai=ai: e.activation(ast[ai][:, 7:8], ast[ai][:, 6:7], AF.Exp, scale=cst[:, C_NEGH:C_NEGH + 1]))
                    wait("scalar", "sXW", xn_stored[su])
                    wait("scalar", "sOU", counts["sOU"])
                    xb_t = inc("scalar", "sLD", 16,
                               lambda e, su=su: e.dma_start(xnb[:, :], xnew_d[su * P:su * P + P, :]),
                               dma=True)
                    wait("vector", "sSR", sr_t)
                    wait("vector", "sST", counts["sST"])
                    raw("vector", lambda e, ai=ai: e.tensor_scalar(
                        outb[:, :], ofb[:, :], ast[ai][:, 2:3], ast[ai][:, 7:8],
                        ALU.subtract, ALU.mult))
                    vdrain()
                    wait("vector", "sLD", xb_t)
                    ou_t = inc("vector", "sOU", 1,
                               lambda e: e.tensor_add(outb[:, :], outb[:, :], xnb[:, :]))
                    wait("scalar", "sOU", ou_t)
                    st_tok[su] = inc("scalar", "sST", 16,
                                     lambda e, su=su: e.dma_start(
                                         out_d[(su - 1) * P:su * P, :], outb[:, :]),
                                     dma=True)

        wait("scalar", "sST", counts["sST"])

        block = ctx.enter_context(nc.Block())

        def runner(oplist):
            def f(e):
                for fn in oplist:
                    fn(e)
            return f

        block.gpsimd(runner(ops["gpsimd"]))
        block.sync(runner(ops["sync"]))
        block.tensor(runner(ops["tensor"]))
        block.vector(runner(ops["vector"]))
        block.scalar(runner(ops["scalar"]))

    return nc


def kernel(L0_attn_x, L0_attn_wq, L0_attn_bq, L0_attn_wk, L0_attn_bk,
           L0_attn_wv, L0_attn_bv, L0_attn_wo, L0_attn_bo,
           L0_ffn_W_r, L0_ffn_W_k, L0_ffn_W_v, L0_ffn_mu_r, L0_ffn_mu_k):
    x = np.ascontiguousarray(np.asarray(L0_attn_x, dtype=np.float32))
    B, N, _ = x.shape
    f32 = lambda a: np.ascontiguousarray(np.asarray(a, dtype=np.float32))
    wq, wk, wv, wo = f32(L0_attn_wq), f32(L0_attn_wk), f32(L0_attn_wv), f32(L0_attn_wo)
    wrt = np.ascontiguousarray(f32(L0_ffn_W_r).T)
    wkt = np.ascontiguousarray(f32(L0_ffn_W_k).T)
    wvt = np.ascontiguousarray(f32(L0_ffn_W_v).T)
    bq, bk, bv, bo = f32(L0_attn_bq), f32(L0_attn_bk), f32(L0_attn_bv), f32(L0_attn_bo)
    mur, muk = f32(L0_ffn_mu_r), f32(L0_ffn_mu_k)

    chunked = lambda v: np.ascontiguousarray(v.reshape(NCH, P).T)
    bcast = lambda v: np.ascontiguousarray(np.broadcast_to(v, (P, D))).astype(ml_dtypes.bfloat16)
    sel = np.zeros((H, D), dtype=ml_dtypes.bfloat16)
    for h in range(H):
        sel[h, h * DH:(h + 1) * DH] = 1
    ones1 = np.ones((P, 1), dtype=ml_dtypes.bfloat16)
    cstv = np.tile(np.array([[1e-5, 1e-6, 1.0, -1.0, -0.5, 1.0 / 1024.0, 0.25, 0.0]],
                            dtype=np.float32), (P, 1))

    seed = (time.time_ns() ^ int.from_bytes(os.urandom(8), "little")) & (2 ** 63 - 1)
    rng = np.random.default_rng(seed)
    n0, n1 = [int(v) for v in rng.integers(1, 2 ** 31 - 1, size=2)]

    nc = build_program()
    in_maps = []
    for c in range(8):
        b = c // 4
        st = (c % 4) * 2048
        fst = st ^ 4096
        halo = x[b, st - 128:st] if st > 0 else np.zeros((128, D), dtype=np.float32)
        xin = np.ascontiguousarray(np.concatenate([halo, x[b, st:st + 2048]], axis=0))
        xforv = np.ascontiguousarray(x[b, fst:fst + 2048])
        hm = np.full((P, 1), 1.0 if st > 0 else 0.0, dtype=np.float32)
        in_maps.append({
            "xin": xin, "xfor": xforv,
            "wq": wq, "wk": wk, "wv": wv, "wo": wo,
            "wrt": wrt, "wkt": wkt, "wvt": wvt,
            "bqs": chunked(bq), "bkb": bcast(bk), "bvb": bcast(bv), "bob": bcast(bo),
            "murs": chunked(mur), "muks": chunked(muk),
            "sel": sel, "ones1": ones1, "cstin": cstv, "hmask": hm,
            "meta": np.array([[c % 2, n0, n1, 0]], dtype=np.int32),
        })

    res = run_bass_kernel_spmd(nc, in_maps, core_ids=list(range(8)))
    out = np.empty((B, N, D), dtype=np.float32)
    for c in range(8):
        b = c // 4
        st = (c % 4) * 2048
        out[b, st:st + 2048] = res.results[c]["out"]
    return out
